# version 12
# baseline (speedup 1.0000x reference)
import numpy as np
from contextlib import ExitStack
import concourse.bass as bass
import concourse.mybir as mybir
from concourse.bass_utils import run_bass_kernel_spmd

F32, BF16, I32, F16 = mybir.dt.float32, mybir.dt.bfloat16, mybir.dt.int32, mybir.dt.float16
ALU = mybir.AluOpType
AF = mybir.ActivationFunctionType

NCORES = 8
S = 2048
D = 4096
INTER = 11008
NI = INTER // 128
NSC = S // 512
NT = S // 128
NCD = 11


class MDict(dict):
    def __missing__(self, key):
        return 0


class SerialVec:
    OPS = {"tensor_tensor", "tensor_scalar", "scalar_tensor_tensor", "tensor_copy",
           "bn_stats", "bn_aggr", "reciprocal", "memset", "tensor_reduce", "select"}

    def __init__(self, v, sem, n0=0):
        self.v = v
        self.sem = sem
        self.n = n0

    def __getattr__(self, name):
        f = getattr(self.v, name)
        if name in SerialVec.OPS:
            def wrap(*a, **k):
                if self.n:
                    self.v.wait_ge(self.sem, self.n)
                inst = f(*a, **k)
                inst.then_inc(self.sem, 1)
                self.n += 1
                return inst
            return wrap
        return f

    def wait_ge(self, sem, val):
        if val > 0:
            self.v.wait_ge(sem, val)


def wge(eng, sem, val):
    if val > 0:
        eng.wait_ge(sem, val)


def build_nc(M, debug=False, phases=4):
    nc = bass.Bass("TRN2", target_bir_lowering=False)
    es = ExitStack()

    dt = lambda n, shp, ty, kind="ExternalInput": nc.dram_tensor(n, shp, ty, kind=kind).ap()
    x = dt("x", [S, D], F32)
    mask = dt("mask", [D, D], I32)
    w = dt("w", [D, D], F32)
    bias = dt("bias", [1, D], F32)
    gq = dt("gate_q", [512, INTER], I32)
    gsx = dt("gate_sx", [512, INTER], F16)
    gs = dt("gate_s", [32, INTER], F16)
    gz = dt("gate_z", [32, INTER // 8], I32)
    uq = dt("up_q", [512, INTER], I32)
    usx = dt("up_sx", [512, INTER], F16)
    us = dt("up_s", [32, INTER], F16)
    uz = dt("up_z", [32, INTER // 8], I32)
    dq = dt("down_q", [1376, D], I32)
    dsx = dt("down_sx", [1376, D], F16)
    ds = dt("down_s", [86, D], F16)
    dz = dt("down_z", [86, 512], I32)
    ident = dt("ident", [128, 128], F16)
    gcin = dt("gc", [128, 56], F16)
    e1in = dt("e1", [128, 256], F16)
    gdin = dt("gd", [128, 166], F16)
    ones1 = dt("ones1", [1, 128], F16)
    zpat32 = dt("zpat32", [32, 8], I32)
    zpatd = dt("zpatd", [86, 8], I32)
    out = dt("out", [S, D], F32, kind="ExternalOutput")
    kd = "ExternalOutput" if debug else "Internal"
    interT = dt("interT", [INTER, S], F16, kind=kd)
    acc = dt("acc", [S, D], F16, kind=kd)
    bmsc = dt("bmsc", [128, 128], F16, kind="Internal")

    sb, psum = nc.sbuf_tensor, nc.psum_tensor
    BEX = es.enter_context(nc.sbuf_tensor("BEX", [128, 32, 128], F16, side="right"))
    x2Tes = ExitStack()
    x2T = x2Tes.enter_context(nc.sbuf_tensor("x2T", [128, 32, S], F16, side="right"))
    xsum_neg = es.enter_context(sb("xsum_neg", [32, S], F16))
    isumn = es.enter_context(sb("isumn", [86, S], F16))
    IDENT = es.enter_context(sb("IDENT", [128, 128], F16))
    GCS = es.enter_context(sb("GCS", [128, 56], F16))
    GDS = es.enter_context(sb("GDS", [128, 166], F16))
    E1S = es.enter_context(sb("E1S", [128, 256], F16))
    ONES1 = es.enter_context(sb("ONES1", [1, 128], F16))
    ZPAT8 = es.enter_context(sb("ZPAT8", [32, 8], I32))
    ZPATD = es.enter_context(sb("ZPATD", [86, 8], I32))

    sem = lambda n: es.enter_context(nc.semaphore(n))
    s_vec = sem("s_vec")
    s_pe = sem("s_pe")
    s_scl = sem("s_scl")
    s_ld = sem("s_ld")
    s_ld2 = sem("s_ld2")
    s_st = sem("s_st")
    s_pre = sem("s_pre")
    s_mm = sem("p2_mm")
    s_sig = sem("p2_sig")
    s_ist = sem("p2_ist")
    s_itc = [sem(f"p2_itc{i}") for i in range(2)]
    s_ldw = [sem("p2_ldg"), sem("p2_ldu")]
    s_ld3 = sem("p3_ld3")
    s_ld4 = sem("p3_ld4")
    s_ld5 = sem("p3_ld5")
    s_out = sem("p3_out")
    s_xt = [sem(f"p0_xt{i}") for i in range(2)]
    s_mk = [sem(f"p1_mk{i}") for i in range(6)]
    s_bex = sem("p1_bexs")
    s_wtp = [sem(f"p1_wtps{i}") for i in range(2)]
    s_xr = [sem(f"p1_xrs{i}") for i in range(4)]
    s_itb = [sem(f"p3_itbs{i}") for i in range(3)]
    s_qd = [sem(f"p3_qds{i}") for i in range(4)]
    s_zpl = sem("p3_zpl")
    s_dbg = sem("s_dbg")
    s_w16 = [sem(f"p1_w16s{i}") for i in range(2)]
    s_bias = sem("p1_bias")
    s_ast = [sem(f"p1_ast{i}") for i in range(4)]
    s_ist4 = [sem(f"p2_ist4{i}") for i in range(4)]
    s_out4 = [sem(f"p3_out4{i}") for i in range(4)]

    cnt = {'pe': 0, 'scl': 0, 'ld': 0, 'ld2': 0, 'st': 0, 'mm': 0, 'ist': 0,
           'itc': 0, 'ld3': 0, 'ld4': 0, 'ld5': 0, 'out': 0, 'ldw0': 0, 'ldw1': 0,
           'sig': 0, 'vec': 0}
    prog = {'sync': [], 'tensor': [], 'vector': [], 'scalar': [], 'gpsimd': []}
    cur_phase = [1]

    def run(name):
        ph = cur_phase[0]
        def deco(f):
            if ph <= phases:
                prog[name].append(f)
            return f
        return deco

    p0 = ExitStack()
    XT = [p0.enter_context(sb(f"p0_xt{b}", [128, D], F32)) for b in range(2)]
    XN = p0.enter_context(sb("p0_xn", [128, D], F16))
    STATS = p0.enter_context(sb("p0_stats", [128, 48], F32))
    MV = [p0.enter_context(sb(f"p0_mv{b}", [128, 2], F32)) for b in range(2)]
    VE = [p0.enter_context(sb(f"p0_ve{b}", [128, 1], F32)) for b in range(2)]
    SD = [p0.enter_context(sb(f"p0_sd{b}", [128, 1], F32)) for b in range(2)]
    RS = p0.enter_context(sb("p0_rs", [128, 1], F32))
    NMR = p0.enter_context(sb("p0_nmr", [128, 1], F32))
    MKB = [p0.enter_context(sb(f"p0_mkb{b}", [128, 1024], F16)) for b in range(6)]
    S1 = p0.enter_context(sb("p0_s1", [128, 8, 512], F16))
    S1T = XN[:].rearrange("p (a b) -> p a b", a=32)
    BM01 = p0.enter_context(sb("p0_bm01", [128, 128], F16))
    PT = [p0.enter_context(psum(f"p0_pt{b}", [128, 1024], F16)) for b in range(4)]
    S1TP = p0.enter_context(psum("p0_s1tp", [128, 1024], F16))
    S1P = [p0.enter_context(psum(f"p0_s1p{b}", [128, 512], F32)) for b in range(2)]
    XS = S1P[1][0:32, :]
    BMP = p0.enter_context(psum("p0_bmp", [128, 128], F32))
    NKS = 8

    @run('sync')
    def _(eng):
        for dst, srcap in [(IDENT[:], ident), (GCS[:], gcin), (GDS[:], gdin), (E1S[:], e1in),
                           (ONES1[:], ones1), (ZPAT8[:], zpat32), (ZPATD[:], zpatd)]:
            eng.dma_start(out=dst, in_=srcap).then_inc(s_pre, 16)
        for t in range(NT):
            if t >= 2:
                wge(eng, s_vec, M[('p0norm', t - 2)])
            eng.dma_start(out=XT[t % 2][:], in_=x[t * 128:(t + 1) * 128, :]).then_inc(s_xt[t % 2], 16)
        wge(eng, s_st, M['p1bmst'])
        for b4 in range(4):
            srcb = bmsc[b4::4, :][None, :, :].to_broadcast([32, 32, 128])
            eng.dma_start(out=BEX[32 * b4:32 * (b4 + 1), :, :], in_=srcb).then_inc(s_bex, 16)
        M['p1bex'] = 4 * 16

    @run('gpsimd')
    def _(eng):
        for k2 in range(NKS // 2 * 32):
            ks2, ncx = k2 // 32, k2 % 32
            if k2 >= 6:
                pv = k2 - 6
                wge(eng, s_pe, M[('p1s1mmB', pv // 32, pv % 32)])
            inap = mask[ncx * 128:(ncx + 1) * 128, ks2 * 1024:(ks2 + 1) * 1024]
            eng.dma_start(out=MKB[k2 % 6][:], in_=inap).then_inc(s_mk[k2 % 6], 16)
            M[('p1mk_ld', k2)] = (k2 // 6 + 1) * 16

    @run('scalar')
    def _(s):
        for t in range(NT):
            wge(s, s_vec, M[('p0ve', t)])
            s.sqrt(out=SD[t % 2][:], in_=VE[t % 2][:]).then_inc(s_scl, 1)
            cnt['scl'] += 1
            M[('p0sd', t)] = cnt['scl']
        wge(s, s_vec, M['p1bm01'])
        s.dma_start(out=bmsc, in_=BM01[:]).then_inc(s_st, 16)
        cnt['st'] += 16
        M['p1bmst'] = cnt['st']

    @run('vector')
    def _(v_raw):
        v = SerialVec(v_raw, s_vec, cnt['vec'])
        s1ks = [0]
        s1tevk = [0]

        def s1tev_upto(lim):
            while s1tevk[0] < lim:
                kc = s1tevk[0]
                v.wait_ge(s_pe, M[('p1s1t', kc)])
                v.tensor_copy(out=S1T[:, kc, :], in_=S1TP[:, (kc % 8) * 128:(kc % 8 + 1) * 128])
                M[('p1s1tev', kc)] = v.n
                s1tevk[0] += 1

        def s1ev_upto(lim):
            while s1ks[0] < lim:
                ks = s1ks[0]
                v.wait_ge(s_pe, M[('p1s1', ks)])
                v.tensor_copy(out=S1[:, ks, :], in_=S1P[ks % 2][:])
                M[('p1s1ev', ks)] = v.n
                s1ks[0] += 1

        for t in range(NT + 1):
            if t < NT:
                v.wait_ge(s_xt[t % 2], (t // 2 + 1) * 16)
                xt = XT[t % 2]
                for c8 in range(8):
                    v.bn_stats(out=STATS[:, c8 * 6:(c8 + 1) * 6], in_=xt[:, c8 * 512:(c8 + 1) * 512])
                v.bn_aggr(out=MV[t % 2][:], in_=STATS[:])
                v.tensor_scalar(out=VE[t % 2][:], in0=MV[t % 2][:, 1:2], scalar1=1e-5,
                                scalar2=None, op0=ALU.add)
                M[('p0ve', t)] = v.n
            if t >= 1:
                tp = t - 1
                xp = XT[tp % 2]
                v.wait_ge(s_scl, M[('p0sd', tp)])
                v.reciprocal(out=RS[:], in_=SD[tp % 2][:])
                v.scalar_tensor_tensor(out=NMR[:], in0=MV[tp % 2][:, 0:1], scalar=-1.0, in1=RS[:],
                                       op0=ALU.mult, op1=ALU.mult)
                if tp >= 1:
                    v.wait_ge(s_pe, M[('p0tp', tp - 1, 3)])
                v.tensor_scalar(out=XN[:], in0=xp[:], scalar1=RS[:], scalar2=NMR[:],
                                op0=ALU.mult, op1=ALU.add)
                M[('p0norm', tp)] = v.n
            if t >= 2:
                for c in range(4):
                    v.wait_ge(s_pe, M[('p0tp', t - 2, c)])
                    v.tensor_copy(out=x2T[:, 8 * c:8 * c + 8, (t - 2) * 128:(t - 1) * 128],
                                  in_=PT[c][:].rearrange("p (q s) -> p q s", q=8))
                    M[('p0ev', t - 2, c)] = v.n
                s1ev_upto(max(0, 2 * (((t - 2) * 8) // 32)))
        for c in range(4):
            v.wait_ge(s_pe, M[('p0tp', NT - 1, c)])
            v.tensor_copy(out=x2T[:, 8 * c:8 * c + 8, (NT - 1) * 128:NT * 128],
                          in_=PT[c][:].rearrange("p (q s) -> p q s", q=8))
            M[('p0ev', NT - 1, c)] = v.n
        s1ev_upto(NKS)
        for sc in range(NSC):
            v.wait_ge(s_pe, M[('p0xs', sc)])
            v.tensor_scalar(out=xsum_neg[:, sc * 512:(sc + 1) * 512], in0=XS,
                            scalar1=-1.0, scalar2=None, op0=ALU.mult)
            M[('p0xsev', sc)] = v.n
        s1tev_upto(32)
        v.wait_ge(s_pe, M['p1bmp'])
        v.tensor_scalar(out=BM01[:], in0=BMP[:], scalar1=1.0, scalar2=None, op0=ALU.min)
        M['p1bm01'] = v.n
        cnt['vec'] = v.n
        M['p0_vec_end'] = v.n

    @run('tensor')
    def _(t_):
        t_.wait_ge(s_pre, 16 * 7)
        mkit = [0]
        s1tkc = [0]

        def s1t_upto(lim):
            lim = min(lim, 8)
            while s1tkc[0] < lim:
                kc = s1tkc[0]
                wge(t_, s_vec, M[('p1s1ev', kc // 4)])
                mm = t_.transpose(out=S1TP[:, (kc % 8) * 128:(kc % 8 + 1) * 128],
                                  in_=S1[:, kc // 4, (kc % 4) * 128:(kc % 4 + 1) * 128],
                                  identity=IDENT[:])
                mm.then_inc(s_pe, 1)
                cnt['pe'] += 1
                M[('p1s1t', kc)] = cnt['pe']
                s1tkc[0] += 1

        def s1t_rest():
            while s1tkc[0] < 32:
                kc = s1tkc[0]
                wge(t_, s_vec, M[('p1s1ev', kc // 4)])
                if kc >= 8:
                    wge(t_, s_vec, M[('p1s1tev', kc - 8)])
                mm = t_.transpose(out=S1TP[:, (kc % 8) * 128:(kc % 8 + 1) * 128],
                                  in_=S1[:, kc // 4, (kc % 4) * 128:(kc % 4 + 1) * 128],
                                  identity=IDENT[:])
                mm.then_inc(s_pe, 1)
                cnt['pe'] += 1
                M[('p1s1t', kc)] = cnt['pe']
                s1tkc[0] += 1
            for kc in range(32):
                wge(t_, s_vec, M[('p1s1tev', kc)])
                mm2 = t_.matmul(out=BMP[:], lhsT=S1T[:, kc, :], rhs=E1S[:, 128 - 4 * kc:256 - 4 * kc],
                                start=(kc == 0), stop=(kc == 31))
            mm2.then_inc(s_pe, 1)
            cnt['pe'] += 1
            M['p1bmp'] = cnt['pe']

        def mask_mm_upto(limp):
            while mkit[0] < limp:
                k2 = mkit[0]
                ks2, ncx = k2 // 32, k2 % 32
                if ncx == 0 and ks2 >= 1:
                    wge(t_, s_vec, M[('p1s1ev', 2 * ks2 - 2)])
                    wge(t_, s_vec, M[('p1s1ev', 2 * ks2 - 1)])
                wge(t_, s_mk[k2 % 6], M[('p1mk_ld', k2)])
                for half in range(2):
                    mm = t_.matmul(out=S1P[half][:], lhsT=E1S[:, 128 - 4 * ncx:256 - 4 * ncx],
                                   rhs=MKB[k2 % 6][:, half * 512:(half + 1) * 512],
                                   start=(ncx == 0), stop=(ncx == 31))
                    mm.then_inc(s_pe, 1)
                    cnt['pe'] += 1
                    M[('p1s1mm' if half == 0 else 'p1s1mmB', ks2, ncx)] = cnt['pe']
                if ncx == 31:
                    M[('p1s1', 2 * ks2)] = cnt['pe'] - 1
                    M[('p1s1', 2 * ks2 + 1)] = cnt['pe']
                mkit[0] += 1

        for t in range(NT):
            mask_mm_upto(max(0, (t - 1) * 8))
            s1t_upto(max(0, min(32, 4 * ((t - 1) * 8 // 32 * 2 - 1))))
            wge(t_, s_vec, M[('p0norm', t)])
            for c in range(4):
                if t >= 1:
                    wge(t_, s_vec, M[('p0ev', t - 1, c)])
                for j in range(8):
                    mm = t_.transpose(out=PT[c][:, j * 128:(j + 1) * 128],
                                      in_=XN[:, 1024 * c + j: 1024 * c + j + 1017: 8],
                                      identity=IDENT[:])
                mm.then_inc(s_pe, 1)
                cnt['pe'] += 1
                M[('p0tp', t, c)] = cnt['pe']
        mask_mm_upto(NKS // 2 * 32)
        for sc in range(NSC):
            wge(t_, s_vec, M[('p0ev', 4 * sc + 3, 3)])
            if sc == 0:
                wge(t_, s_vec, M[('p1s1ev', NKS - 1)])
            if sc >= 1:
                wge(t_, s_vec, M[('p0xsev', sc - 1)])
            first = True
            for c in range(4):
                for j in range(8):
                    mm = t_.matmul(out=XS, lhsT=GCS[:, 24 - 8 * c:56 - 8 * c],
                                   rhs=x2T[:, 8 * c + j, sc * 512:(sc + 1) * 512],
                                   start=first, stop=(c == 3 and j == 7))
                    first = False
            mm.then_inc(s_pe, 1)
            cnt['pe'] += 1
            M[('p0xs', sc)] = cnt['pe']
        s1t_rest()
        M['p0_pe_end'] = cnt['pe']
    p0.close()

    cur_phase[0] = 2
    p1 = ExitStack()
    W16 = [p1.enter_context(sb(f"p1_w16{b}", [128, D], F16)) for b in range(2)]
    WTP = p1.enter_context(sb("p1_wtp", [128, 32, 512], F16))
    BIAS16 = p1.enter_context(sb("p1_bias16", [1, D], F16))
    XR = [p1.enter_context(sb(f"p1_xr{b}", [128, 512], F32)) for b in range(2)]
    ACCB = [p1.enter_context(sb(f"p1_accb{b}", [128, 512], F16)) for b in range(2)]
    PT1 = [p1.enter_context(psum(f"p1_pt{b}", [128, 1024], F16)) for b in range(2)]
    AT = [p1.enter_context(psum(f"p1_at{b}", [128, 512], F32)) for b in range(4)]

    @run('gpsimd')
    def _(eng):
        wge(eng, s_vec, M['p0_vec_end'])
        wge(eng, s_pe, M['p0_pe_end'])
        wge(eng, s_st, M['p1bmst'])
        eng.dma_start(out=BIAS16[:], in_=bias).then_inc(s_bias, 16)
        for nt in range(32):
            if nt >= 2:
                wge(eng, s_pe, M[('p1wt', nt - 2, 3)])
            eng.dma_start(out=W16[nt % 2][:], in_=w[nt * 128:(nt + 1) * 128, :]).then_inc(s_w16[nt % 2], 16)
            M[('p1w_ld', nt)] = (nt // 2 + 1) * 16

    @run('sync')
    def _(eng):
        wge(eng, s_vec, M['p0_vec_end'])
        wge(eng, s_pe, M['p0_pe_end'])
        wge(eng, s_st, M['p1bmst'])
        for nb in range(8):
            for st in range(NT):
                it = nb * NT + st
                if it >= 2:
                    pv = it - 2
                    wge(eng, s_vec, M[('p1acc', pv // NT, pv % NT)])
                eng.dma_start(out=XR[st % 2][:],
                              in_=x[st * 128:(st + 1) * 128, nb * 512:(nb + 1) * 512]).then_inc(s_xr[st % 2], 16)
                M[('p1xr', nb, st)] = ((nb * NT + st) // 2 + 1) * 16

    @run('vector')
    def _(v_raw):
        v = SerialVec(v_raw, s_vec, cnt['vec'])
        v.wait_ge(s_bex, M['p1bex'])
        for nb in range(8):
            for nti in range(4):
                nt = nb * 4 + nti
                v.wait_ge(s_w16[nt % 2], M[('p1w_ld', nt)])
                v.tensor_tensor(out=W16[nt % 2][:].rearrange("p (kb r) -> p kb r", r=32),
                                in0=W16[nt % 2][:].rearrange("p (kb r) -> p kb r", r=32),
                                in1=BEX[:, nt, :][:, :, None].to_broadcast([128, 128, 32]),
                                op=ALU.mult)
                M[('p1wm', nt)] = v.n
                for c in range(4):
                    v.wait_ge(s_pe, M[('p1wt', nt, c)])
                    v.tensor_copy(out=WTP[:, 8 * c:8 * c + 8, nti * 128:(nti + 1) * 128],
                                  in_=PT1[c % 2][:].rearrange("p (q n) -> p q n", q=8))
                    M[('p1wtev', nt, c)] = v.n
            for st in range(NT):
                v.wait_ge(s_pe, M[('p1at', nb, st)])
                v.wait_ge(s_xr[st % 2], M[('p1xr', nb, st)])
                if nb * NT + st >= 2:
                    pv = nb * NT + st - 2
                    v.wait_ge(s_ast[pv % 4], (pv // 4 + 1) * 16)
                v.tensor_tensor(out=ACCB[st % 2][:], in0=AT[st % 4][:], in1=XR[st % 2][:], op=ALU.add)
                M[('p1acc', nb, st)] = v.n
        cnt['vec'] = v.n
        M['p1_vec_end'] = v.n

    @run('tensor')
    def _(t_):
        t_.wait_ge(s_bias, 16)
        for nb in range(8):
            for nti in range(4):
                nt = nb * 4 + nti
                wge(t_, s_vec, M[('p1wm', nt)])
                for c in range(4):
                    if nt * 4 + c >= 2:
                        pv = nt * 4 + c - 2
                        wge(t_, s_vec, M[('p1wtev', pv // 4, pv % 4)])
                    for j in range(8):
                        mm = t_.transpose(out=PT1[c % 2][:, j * 128:(j + 1) * 128],
                                          in_=W16[nt % 2][:, 1024 * c + j: 1024 * c + j + 1017: 8],
                                          identity=IDENT[:])
                    mm.then_inc(s_pe, 1)
                    cnt['pe'] += 1
                    M[('p1wt', nt, c)] = cnt['pe']
            wge(t_, s_vec, M[('p1wtev', nb * 4 + 3, 3)])
            for st in range(NT):
                if nb * NT + st >= 4:
                    pv = nb * NT + st - 4
                    wge(t_, s_vec, M[('p1acc', pv // NT, pv % NT)])
                t_.matmul(out=AT[st % 4][:], lhsT=ONES1[:], rhs=BIAS16[:, nb * 512:(nb + 1) * 512],
                          start=True, stop=False)
                for c in range(4):
                    for j in range(8):
                        q = 8 * c + j
                        mm = t_.matmul(out=AT[st % 4][:], lhsT=x2T[:, q, st * 128:(st + 1) * 128],
                                       rhs=WTP[:, q, :], start=False, stop=(c == 3 and j == 7))
                mm.then_inc(s_pe, 1)
                cnt['pe'] += 1
                M[('p1at', nb, st)] = cnt['pe']
        M['p1_pe_end'] = cnt['pe']

    @run('scalar')
    def _(s):
        for nb in range(8):
            for st in range(NT):
                wge(s, s_vec, M[('p1acc', nb, st)])
                s.dma_start(out=acc[st * 128:(st + 1) * 128, nb * 512:(nb + 1) * 512],
                            in_=ACCB[st % 2][:]).then_inc(s_ast[st % 4], 16)
        M['p1_st_end'] = cnt['st']
    p1.close()

    cur_phase[0] = 3
    p2 = ExitStack()
    Q = [p2.enter_context(sb(f"p2_q{ww}", [128, 512], I32)) for ww in range(2)]
    SX = [p2.enter_context(sb(f"p2_sx{ww}", [128, 512], F16)) for ww in range(2)]
    SSB = [p2.enter_context(sb(f"p2_ss{ww}", [32, 128], F16)) for ww in range(2)]
    ZSB = [p2.enter_context(sb(f"p2_zs{ww}", [32, 16], I32)) for ww in range(2)]
    ZPB = [p2.enter_context(sb(f"p2_zp{ww}", [32, 128], F16)) for ww in range(2)]
    ZPI = p2.enter_context(sb("p2_zpi", [32, 128], I32))
    NIB = p2.enter_context(sb("p2_nib", [128, 512], I32))
    PL = [p2.enter_context(sb(f"p2_pl{ww}", [128, 8, 512], F16)) for ww in range(2)]
    GATE = p2.enter_context(sb("p2_gate", [128, NSC, 512], F16))
    UPB = p2.enter_context(sb("p2_upb", [128, NSC, 512], F16))
    SIGB = p2.enter_context(sb("p2_sig", [128, NSC, 512], F16))
    T1 = p2.enter_context(sb("p2_t1", [128, 512], F32))
    INT = [p2.enter_context(sb(f"p2_int{b}", [128, 512], F16)) for b in range(4)]
    ITC = [p2.enter_context(sb(f"p2_itc{b}", [128, 8, 512], F16)) for b in range(2)]
    ACC2 = [p2.enter_context(psum(f"p2_acc{i}", [128, 512], F32)) for i in range(4)]
    ISUMP = [p2.enter_context(psum(f"p2_isum{i}", [86, 512], F32)) for i in range(4)]

    qw_ = [gq, uq]
    sxw_ = [gsx, usx]
    sw_ = [gs, us]
    zw_ = [gz, uz]
    MM2 = lambda ii, ww, sc: (ii * 2 + ww) * NSC + sc + 1
    PSC2 = lambda ii, sc: ii * NSC + sc + 1

    @run('sync')
    def _(eng):
        wge(eng, s_vec, M['p1_vec_end'])
        wge(eng, s_pe, M['p1_pe_end'])
        for sl in range(4):
            eng.wait_ge(s_ast[sl], 32 * 16)
        for ii in range(NI):
            for ww in range(2):
                if ii >= 1:
                    wge(eng, s_vec, M[('p2deq', ii - 1, ww)])
                eng.dma_start(out=Q[ww][:].rearrange("p (c i) -> p c i", c=4),
                              in_=qw_[ww].rearrange("(c p) i -> p c i", p=128)[:, :, ii * 128:(ii + 1) * 128]).then_inc(s_ldw[ww], 16)
                eng.dma_start(out=SX[ww][:].rearrange("p (c i) -> p c i", c=4),
                              in_=sxw_[ww].rearrange("(c p) i -> p c i", p=128)[:, :, ii * 128:(ii + 1) * 128]).then_inc(s_ldw[ww], 16)
                eng.dma_start(out=SSB[ww][:], in_=sw_[ww][:, ii * 128:(ii + 1) * 128]).then_inc(s_ldw[ww], 16)
                eng.dma_start(out=ZSB[ww][:], in_=zw_[ww][:, ii * 16:(ii + 1) * 16]).then_inc(s_ldw[ww], 16)


    @run('gpsimd')
    def _(eng):
        for cp in range(NCD):
            rows = 128 if cp < 10 else 96
            hi = min((8 * cp + 8), NI) * 16
            for sc in range(NSC):
                k = cp * NSC + sc
                for sl in range(4):
                    wge(eng, s_ist4[sl], hi)
                if k >= 2:
                    pv = k - 2
                    wge(eng, s_pe, M[('p2is', pv // NSC, pv % NSC)])
                inap = interT[cp * 1024: cp * 1024 + rows * 8, sc * 512:(sc + 1) * 512]
                inap = inap.rearrange("(p j) s -> p j s", j=8)
                eng.dma_start(out=ITC[k % 2][0:rows, :, :], in_=inap).then_inc(s_itc[k % 2], 16)
                M[('p2itc', cp, sc)] = (k // 2 + 1) * 16

    @run('vector')
    def _(v_raw):
        v = SerialVec(v_raw, s_vec, cnt['vec'])

        def deq2(ii, ww):
            v.wait_ge(s_ldw[ww], (ii + 1) * 64)
            zin = ZSB[ww][:, :, None].to_broadcast([32, 16, 8])
            v.tensor_tensor(out=ZPI[:].rearrange("g (a c) -> g a c", c=8), in0=zin,
                            in1=ZPAT8[:, None, :].to_broadcast([32, 16, 8]), op=ALU.logical_shift_right)
            v.tensor_scalar(out=ZPI[:], in0=ZPI[:], scalar1=15, scalar2=None, op0=ALU.bitwise_and)
            v.scalar_tensor_tensor(out=ZPB[ww][:], in0=ZPI[:], scalar=1.0,
                                   in1=SSB[ww][:], op0=ALU.add, op1=ALU.mult)
            for j in range(8):
                v.tensor_scalar(out=NIB[:], in0=Q[ww][:], scalar1=4 * j, scalar2=15,
                                op0=ALU.logical_shift_right, op1=ALU.bitwise_and)
                v.scalar_tensor_tensor(out=PL[ww][:, j, :], in0=NIB[:], scalar=1.0,
                                       in1=SX[ww][:], op0=ALU.mult, op1=ALU.mult)
            M[('p2deq', ii, ww)] = v.n

        deq2(0, 0)
        for ii in range(NI):
            deq2(ii, 1)
            for sc in range(NSC):
                v.wait_ge(s_mm, MM2(ii, 0, sc))
                v.tensor_copy(out=GATE[:, sc, :], in_=ACC2[sc][:])
                M[('p2gev', ii, sc)] = v.n
            if ii + 1 < NI:
                deq2(ii + 1, 0)
            for sc in range(NSC):
                v.wait_ge(s_mm, MM2(ii, 1, sc))
                v.tensor_copy(out=UPB[:, sc, :], in_=ACC2[sc][:])
                M[('p2uev', ii, sc)] = v.n
            for sc in range(NSC):
                slot = (ii * NSC + sc) % 4
                v.wait_ge(s_sig, PSC2(ii, sc))
                v.tensor_tensor(out=T1[:], in0=GATE[:, sc, :], in1=SIGB[:, sc, :], op=ALU.mult)
                if ii * NSC + sc >= 4:
                    v.wait_ge(s_ist4[slot], ((ii * NSC + sc) // 4) * 16)
                v.tensor_tensor(out=INT[slot][:], in0=T1[:], in1=UPB[:, sc, :], op=ALU.mult)
                M[('p2int', ii, sc)] = v.n
        for sc in range(NSC):
            v.wait_ge(s_pe, M[('p2is', NCD - 1, sc)])
            v.tensor_scalar(out=isumn[:, sc * 512:(sc + 1) * 512], in0=ISUMP[sc][:],
                            scalar1=-1.0, scalar2=None, op0=ALU.mult)
            M[('p2isev', sc)] = v.n
        cnt['vec'] = v.n
        M['p2_vec_end'] = v.n

    def _p2_isum(t_, cp):
        rows = 128 if cp < 10 else 96
        for sc in range(NSC):
            wge(t_, s_itc[(cp * NSC + sc) % 2], M[('p2itc', cp, sc)])
            for j in range(8):
                mm = t_.matmul(out=ISUMP[sc][:], lhsT=GDS[0:rows, 80 - 8 * cp:166 - 8 * cp],
                               rhs=ITC[(cp * NSC + sc) % 2][0:rows, j, :],
                               start=(cp == 0 and j == 0), stop=(cp == NCD - 1 and j == 7))
            mm.then_inc(s_pe, 1)
            cnt['pe'] += 1
            M[('p2is', cp, sc)] = cnt['pe']

    @run('tensor')
    def _(t_):
        for ii in range(NI):
            for ww in range(2):
                wge(t_, s_vec, M[('p2deq', ii, ww)])
                for sc in range(NSC):
                    if ww == 1:
                        wge(t_, s_vec, M[('p2gev', ii, sc)])
                    elif ii >= 1:
                        wge(t_, s_vec, M[('p2uev', ii - 1, sc)])
                    first = True
                    for c in range(4):
                        for j in range(8):
                            t_.matmul(out=ACC2[sc][:], lhsT=PL[ww][:, j, c * 128:(c + 1) * 128],
                                      rhs=x2T[:, 8 * c + j, sc * 512:(sc + 1) * 512],
                                      start=first, stop=False)
                            first = False
                    t_.matmul(out=ACC2[sc][:], lhsT=ZPB[ww][:],
                              rhs=xsum_neg[:, sc * 512:(sc + 1) * 512],
                              start=False, stop=True).then_inc(s_mm, 1)
                    cnt['mm'] += 1
            if ii % 8 == 7 and ii // 8 >= 1:
                _p2_isum(t_, ii // 8 - 1)
        for cp in [9, 10]:
            _p2_isum(t_, cp)
        M['p2_pe_end'] = cnt['pe']

    @run('scalar')
    def _(s):
        for ii in range(NI):
            for sc in range(NSC):
                wge(s, s_vec, M[('p2gev', ii, sc)])
                s.activation(out=SIGB[:, sc, :], in_=GATE[:, sc, :], func=AF.Sigmoid).then_inc(s_sig, 1)
                cnt['sig'] += 1
            for sc in range(NSC):
                slot = (ii * NSC + sc) % 4
                wge(s, s_vec, M[('p2int', ii, sc)])
                s.dma_start(out=interT[ii * 128:(ii + 1) * 128, sc * 512:(sc + 1) * 512],
                            in_=INT[slot][:]).then_inc(s_ist4[slot], 16)
    p2.close()

    cur_phase[0] = 4
    x2Tes.close()
    NQD, NIT, NPLD = 4, 3, 8
    p3 = ExitStack()
    ITB = [p3.enter_context(sb(f"p3_itb{b}", [128, 8, 1024], F16)) for b in range(NIT)]
    QD = [p3.enter_context(sb(f"p3_qd{b}", [128, 512], I32)) for b in range(NQD)]
    DSE = [p3.enter_context(sb(f"p3_dse{b}", [128, 512], F16)) for b in range(NQD)]
    PLD = [p3.enter_context(sb(f"p3_pld{b}", [128, 512], F16)) for b in range(NPLD)]
    NIBD = p3.enter_context(sb("p3_nibd", [128, 512], I32))
    SDS = p3.enter_context(sb("p3_sds", [86, 512], F16))
    ZDS = p3.enter_context(sb("p3_zds", [86, 64], I32))
    ZDI = p3.enter_context(sb("p3_zdi", [86, 512], I32))
    ZPD = p3.enter_context(sb("p3_zpd", [86, 512], F16))
    ACC3 = p3.enter_context(sb("p3_acc", [128, 8, 512], F16))
    OUTB = [p3.enter_context(sb(f"p3_outb{b}", [128, 512], F32)) for b in range(4)]
    OUT = [p3.enter_context(psum(f"p3_out{b}", [128, 512], F32)) for b in range(8)]

    @run('sync')
    def _(eng):
        wge(eng, s_vec, M['p2_vec_end'])
        wge(eng, s_pe, M['p2_pe_end'])
        for sl in range(4):
            eng.wait_ge(s_ist4[sl], NI * 16)
        qd_prev = {i: None for i in range(NQD)}
        itb_prev = {i: None for i in range(NIT)}
        qd_cnt = {i: 0 for i in range(NQD)}
        itb_cnt = {i: 0 for i in range(NIT)}
        for hb in range(8):
            if hb >= 1:
                wge(eng, s_vec, M[('p3zpd', hb - 1)])
            eng.dma_start(out=SDS[:], in_=ds[:, hb * 512:(hb + 1) * 512]).then_inc(s_zpl, 16)
            eng.dma_start(out=ZDS[:], in_=dz[:, hb * 64:(hb + 1) * 64]).then_inc(s_zpl, 16)
            M[('p3zp_ld', hb)] = (hb + 1) * 32
            for sh in range(2):
                for cp in range(NCD):
                    rows = 128 if cp < 10 else 96
                    it = (hb * 2 + sh) * NCD + cp
                    sl = it % NQD
                    prevq = qd_prev[sl]
                    if prevq is not None:
                        wge(eng, s_vec, M[('p3deq', *prevq)])
                        wge(eng, s_pe, M[('p3mm', *prevq)])
                    qd_prev[sl] = (hb, sh, cp)
                    eng.dma_start(out=QD[sl][0:rows, :],
                                  in_=dq[cp * 128:cp * 128 + rows, hb * 512:(hb + 1) * 512]).then_inc(s_qd[sl], 16)
                    eng.dma_start(out=DSE[sl][0:rows, :],
                                  in_=dsx[cp * 128:cp * 128 + rows, hb * 512:(hb + 1) * 512]).then_inc(s_qd[sl], 16)
                    qd_cnt[sl] += 32
                    M[('p3qd_ld', hb, sh, cp)] = qd_cnt[sl]
                    sli = it % NIT
                    previ = itb_prev[sli]
                    if previ is not None:
                        wge(eng, s_pe, M[('p3mm', *previ)])
                    itb_prev[sli] = (hb, sh, cp)
                    inap = interT[cp * 1024:cp * 1024 + rows * 8, sh * 1024:(sh + 1) * 1024]
                    inap = inap.rearrange("(p j) s -> p j s", j=8)
                    eng.dma_start(out=ITB[sli][0:rows, :, :], in_=inap).then_inc(s_itb[sli], 16)
                    itb_cnt[sli] += 16
                    M[('p3itb', hb, sh, cp)] = itb_cnt[sli]
        for sl in range(4):
            eng.wait_ge(s_out4[sl], 32 * 16)

    @run('gpsimd')
    def _(eng):
        wge(eng, s_vec, M['p2_vec_end'])
        for sl in range(4):
            eng.wait_ge(s_ast[sl], 32 * 16)
            eng.wait_ge(s_ist4[sl], NI * 16)
        for hb in range(8):
            for sh in range(2):
                it = hb * 2 + sh
                if it >= 1:
                    pv = it - 1
                    wge(eng, s_vec, M[('p3ev', pv // 2, pv % 2, 7)])
                inap = acc[sh * 1024:(sh + 1) * 1024, hb * 512:(hb + 1) * 512].rearrange("(st p) h -> p st h", st=8)
                eng.dma_start(out=ACC3[:], in_=inap).then_inc(s_ld5, 16)
                cnt['ld5'] += 16
                M[('p3acc_ld', hb, sh)] = cnt['ld5']

    @run('vector')
    def _(v_raw):
        v = SerialVec(v_raw, s_vec, cnt['vec'])

        def zpd3(hb):
            v.wait_ge(s_zpl, M[('p3zp_ld', hb)])
            zin = ZDS[:, :, None].to_broadcast([86, 64, 8])
            pin = ZPATD[:, None, :].to_broadcast([86, 64, 8])
            v.tensor_tensor(out=ZDI[:].rearrange("g (a c) -> g a c", c=8), in0=zin, in1=pin,
                            op=ALU.logical_shift_right)
            v.tensor_scalar(out=ZDI[:], in0=ZDI[:], scalar1=15, scalar2=None, op0=ALU.bitwise_and)
            v.scalar_tensor_tensor(out=ZPD[:], in0=ZDI[:], scalar=1.0, in1=SDS[:],
                                   op0=ALU.add, op1=ALU.mult)
            M[('p3zpd', hb)] = v.n

        def deq3(H, cp):
            hb, sh = H // 2, H % 2
            rows = 128 if cp < 10 else 96
            it = H * NCD + cp
            v.wait_ge(s_qd[it % NQD], M[('p3qd_ld', hb, sh, cp)])
            for j in range(8):
                slot = (it * 8 + j) % NPLD
                pj = it * 8 + j - NPLD
                if pj >= 0:
                    key = ('p3mmj', pj // (NCD * 8 * 2), (pj // (NCD * 8)) % 2,
                           (pj % (NCD * 8)) // 8, pj % 8)
                    v.wait_ge(s_pe, M[key])
                v.tensor_scalar(out=NIBD[0:rows, :], in0=QD[it % NQD][0:rows, :],
                                scalar1=4 * j, scalar2=15,
                                op0=ALU.logical_shift_right, op1=ALU.bitwise_and)
                v.scalar_tensor_tensor(out=PLD[slot][0:rows, :], in0=NIBD[0:rows, :],
                                       scalar=1.0, in1=DSE[it % NQD][0:rows, :],
                                       op0=ALU.mult, op1=ALU.mult)
                M[('p3deqj', hb, sh, cp, j)] = v.n
            M[('p3deq', hb, sh, cp)] = v.n

        zpd3(0)
        deq3(0, 0)
        for H in range(16):
            hb, sh = H // 2, H % 2
            for cp in range(1, NCD):
                deq3(H, cp)
            if H + 1 < 16:
                deq3(H + 1, 0)
            for st in range(8):
                v.wait_ge(s_pe, M[('p3corr', hb, sh, st)])
                v.wait_ge(s_ld5, M[('p3acc_ld', hb, sh)])
                slot = st % 4
                gidx = H * 8 + st
                if gidx >= 4:
                    v.wait_ge(s_out4[gidx % 4], (gidx // 4) * 16)
                v.tensor_tensor(out=OUTB[slot][:], in0=OUT[st][:], in1=ACC3[:, st, :],
                                op=ALU.add)
                M[('p3ev', hb, sh, st)] = v.n
            if H + 1 < 16 and (H + 1) % 2 == 0:
                zpd3((H + 1) // 2)
        cnt['vec'] = v.n

    @run('tensor')
    def _(t_):
        for hb in range(8):
            for sh in range(2):
                for cp in range(NCD):
                    rows = 128 if cp < 10 else 96
                    it = (hb * 2 + sh) * NCD + cp
                    wge(t_, s_itb[it % NIT], M[('p3itb', hb, sh, cp)])
                    for j in range(8):
                        slot = (it * 8 + j) % NPLD
                        wge(t_, s_vec, M[('p3deqj', hb, sh, cp, j)])
                        for st in range(8):
                            if cp == 0 and j == 0 and hb * 2 + sh >= 1:
                                pv = hb * 2 + sh - 1
                                wge(t_, s_vec, M[('p3ev', pv // 2, pv % 2, st)])
                            mm = t_.matmul(out=OUT[st][:],
                                           lhsT=ITB[it % NIT][0:rows, j, st * 128:(st + 1) * 128],
                                           rhs=PLD[slot][0:rows, :],
                                           start=(cp == 0 and j == 0), stop=False)
                        mm.then_inc(s_pe, 1)
                        cnt['pe'] += 1
                        M[('p3mmj', hb, sh, cp, j)] = cnt['pe']
                    M[('p3mm', hb, sh, cp)] = cnt['pe']
                wge(t_, s_vec, M[('p3zpd', hb)])
                if hb == 0 and sh == 0:
                    for sc in range(NSC):
                        wge(t_, s_vec, M[('p2isev', sc)])
                for st in range(8):
                    gcol = sh * 1024 + st * 128
                    mm = t_.matmul(out=OUT[st][:], lhsT=isumn[:, gcol:gcol + 128], rhs=ZPD[:],
                                   start=False, stop=True)
                    mm.then_inc(s_pe, 1)
                    cnt['pe'] += 1
                    M[('p3corr', hb, sh, st)] = cnt['pe']

    @run('scalar')
    def _(s):
        for hb in range(8):
            for sh in range(2):
                for st in range(8):
                    wge(s, s_vec, M[('p3ev', hb, sh, st)])
                    slot = st % 4
                    s.dma_start(out=out[sh * 1024 + st * 128: sh * 1024 + (st + 1) * 128,
                                        hb * 512:(hb + 1) * 512],
                                in_=OUTB[slot][:]).then_inc(s_out4[(hb * 16 + sh * 8 + st) % 4], 16)
    p3.close()

    cur_phase[0] = 1
    if debug:
        x2Td = dt("x2Td", [128, 32, S], F16, kind="ExternalOutput")
        xsnd = dt("xsnd", [32, S], F16, kind="ExternalOutput")
        isnd = dt("isnd", [86, S], F16, kind="ExternalOutput")

        @run('gpsimd')
        def _(eng):
            wge(eng, s_vec, M['p0_vec_end'])
            if phases >= 2:
                for sl in range(4):
                    eng.wait_ge(s_ast[sl], 32 * 16)
            if phases >= 3:
                for sl in range(4):
                    eng.wait_ge(s_ist4[sl], NI * 16)
                wge(eng, s_vec, M['p2_vec_end'])
            if phases >= 4:
                for sl in range(4):
                    eng.wait_ge(s_out4[sl], 32 * 16)
            ndma = 0
            if phases < 4:
                eng.dma_start(out=x2Td, in_=x2T[:]).then_inc(s_dbg, 16)
                ndma += 1
            eng.dma_start(out=xsnd, in_=xsum_neg[:]).then_inc(s_dbg, 16)
            ndma += 1
            if phases >= 3:
                eng.dma_start(out=isnd, in_=isumn[:]).then_inc(s_dbg, 16)
                ndma += 1
            eng.wait_ge(s_dbg, 16 * ndma)

    blk = es.enter_context(nc.Block())

    @blk.vector
    def _(v):
        for f in prog['vector']:
            f(v)

    @blk.tensor
    def _(t):
        for f in prog['tensor']:
            f(t)

    @blk.scalar
    def _(s):
        for f in prog['scalar']:
            f(s)

    @blk.sync
    def _(sy):
        for f in prog['sync']:
            f(sy)

    @blk.gpsimd
    def _(g):
        for f in prog['gpsimd']:
            f(g)

    es.close()
    return nc


def make_consts():
    f16 = np.float16
    ident = np.eye(128, dtype=np.float32).astype(f16)
    gc = np.zeros((128, 56), dtype=np.float32)
    e1 = np.zeros((128, 256), dtype=np.float32)
    gd = np.zeros((128, 166), dtype=np.float32)
    for p in range(128):
        gc[p, 24 + p // 16] = 1.0
        e1[p, 128 + p // 32] = 1.0
        gd[p, 80 + p // 16] = 1.0
    ones1 = np.ones((1, 128), dtype=np.float32)
    zpat32 = np.tile((np.arange(8) * 4).astype(np.int32), (32, 1))
    zpatd = np.tile((np.arange(8) * 4).astype(np.int32), (86, 1))
    return {"ident": ident, "gc": gc.astype(f16), "e1": e1.astype(f16),
            "gd": gd.astype(f16), "ones1": ones1.astype(f16),
            "zpat32": zpat32, "zpatd": zpatd}


def kernel(L0_attn_data, L0_attn_mask, L0_attn_weight, L0_attn_bias,
           L0_ffn_gate_qweight, L0_ffn_gate_scales, L0_ffn_gate_zeros,
           L0_ffn_up_qweight, L0_ffn_up_scales, L0_ffn_up_zeros,
           L0_ffn_down_qweight, L0_ffn_down_scales, L0_ffn_down_zeros,
           _debug=False, _phases=4):
    xfull = np.ascontiguousarray(np.asarray(L0_attn_data, dtype=np.float32).reshape(16384, D))
    shared = {
        "mask": np.ascontiguousarray(np.asarray(L0_attn_mask, dtype=np.int32)),
        "w": np.ascontiguousarray(np.asarray(L0_attn_weight, dtype=np.float32)),
        "bias": np.ascontiguousarray(np.asarray(L0_attn_bias, dtype=np.float32).reshape(1, D)),
        "gate_q": np.ascontiguousarray(np.asarray(L0_ffn_gate_qweight)),
        "gate_sx": np.ascontiguousarray(np.repeat(np.asarray(L0_ffn_gate_scales), 16, axis=0)),
        "gate_s": np.ascontiguousarray(np.asarray(L0_ffn_gate_scales)),
        "gate_z": np.ascontiguousarray(np.asarray(L0_ffn_gate_zeros)),
        "up_q": np.ascontiguousarray(np.asarray(L0_ffn_up_qweight)),
        "up_sx": np.ascontiguousarray(np.repeat(np.asarray(L0_ffn_up_scales), 16, axis=0)),
        "up_s": np.ascontiguousarray(np.asarray(L0_ffn_up_scales)),
        "up_z": np.ascontiguousarray(np.asarray(L0_ffn_up_zeros)),
        "down_q": np.ascontiguousarray(np.asarray(L0_ffn_down_qweight)),
        "down_sx": np.ascontiguousarray(np.repeat(np.asarray(L0_ffn_down_scales), 16, axis=0)),
        "down_s": np.ascontiguousarray(np.asarray(L0_ffn_down_scales)),
        "down_z": np.ascontiguousarray(np.asarray(L0_ffn_down_zeros)),
    }
    shared.update(make_consts())
    M = MDict()
    build_nc(M, debug=_debug, phases=_phases)
    nc = build_nc(dict(M), debug=_debug, phases=_phases)
    in_maps = []
    for c in range(NCORES):
        m = dict(shared)
        m["x"] = np.ascontiguousarray(xfull[c * S:(c + 1) * S, :])
        in_maps.append(m)
    res = run_bass_kernel_spmd(nc, in_maps, core_ids=list(range(NCORES)))
    outs = [res.results[c]["out"] for c in range(NCORES)]
    full = np.concatenate(outs, axis=0).reshape(1, 16384, D).astype(np.float32)
    if _debug:
        return full, res
    return full


# version 13 (final)
# speedup vs baseline: 1.0936x; 1.0936x over previous
import numpy as np
from contextlib import ExitStack
import concourse.bass as bass
import concourse.mybir as mybir
from concourse.bass_utils import run_bass_kernel_spmd

F32, BF16, I32, F16 = mybir.dt.float32, mybir.dt.bfloat16, mybir.dt.int32, mybir.dt.float16
ALU = mybir.AluOpType
AF = mybir.ActivationFunctionType

NCORES = 8
S = 2048
D = 4096
INTER = 11008
NI = INTER // 128
NSC = S // 512
NT = S // 128
NCD = 11


class MDict(dict):
    def __missing__(self, key):
        return 0


class SerialVec:
    OPS = {"tensor_tensor", "tensor_scalar", "scalar_tensor_tensor", "tensor_copy",
           "bn_stats", "bn_aggr", "reciprocal", "memset", "tensor_reduce", "select"}

    def __init__(self, v, sem, n0=0):
        self.v = v
        self.sem = sem
        self.n = n0

    def __getattr__(self, name):
        f = getattr(self.v, name)
        if name in SerialVec.OPS:
            def wrap(*a, **k):
                if self.n:
                    self.v.wait_ge(self.sem, self.n)
                inst = f(*a, **k)
                inst.then_inc(self.sem, 1)
                self.n += 1
                return inst
            return wrap
        return f

    def wait_ge(self, sem, val):
        if val > 0:
            self.v.wait_ge(sem, val)


def wge(eng, sem, val):
    if val > 0:
        eng.wait_ge(sem, val)


def build_nc(M, debug=False, phases=4):
    nc = bass.Bass("TRN2", target_bir_lowering=False)
    es = ExitStack()

    dt = lambda n, shp, ty, kind="ExternalInput": nc.dram_tensor(n, shp, ty, kind=kind).ap()
    x = dt("x", [S, D], F32)
    mask = dt("mask", [D, D], I32)
    w = dt("w", [D, D], F32)
    bias = dt("bias", [1, D], F32)
    gq = dt("gate_q", [512, INTER], I32)
    gsx = dt("gate_sx", [512, INTER], F16)
    gs = dt("gate_s", [32, INTER], F16)
    gz = dt("gate_z", [32, INTER // 8], I32)
    uq = dt("up_q", [512, INTER], I32)
    usx = dt("up_sx", [512, INTER], F16)
    us = dt("up_s", [32, INTER], F16)
    uz = dt("up_z", [32, INTER // 8], I32)
    dq = dt("down_q", [1376, D], I32)
    dsx = dt("down_sx", [1376, D], F16)
    ds = dt("down_s", [86, D], F16)
    dz = dt("down_z", [86, 512], I32)
    ident = dt("ident", [128, 128], F16)
    gcin = dt("gc", [128, 56], F16)
    e1in = dt("e1", [128, 256], F16)
    gdin = dt("gd", [128, 166], F16)
    ones1 = dt("ones1", [1, 128], F16)
    zpat32 = dt("zpat32", [32, 8], I32)
    zpatd = dt("zpatd", [86, 8], I32)
    out = dt("out", [S, D], F32, kind="ExternalOutput")
    kd = "ExternalOutput" if debug else "Internal"
    interT = dt("interT", [INTER, S], F16, kind=kd)
    acc = dt("acc", [S, D], F16, kind=kd)
    bmsc = dt("bmsc", [128, 128], F16, kind="Internal")

    sb, psum = nc.sbuf_tensor, nc.psum_tensor
    BEX = es.enter_context(nc.sbuf_tensor("BEX", [128, 32, 128], F16, side="right"))
    x2Tes = ExitStack()
    x2T = x2Tes.enter_context(nc.sbuf_tensor("x2T", [128, 32, S], F16, side="right"))
    xsum_neg = es.enter_context(sb("xsum_neg", [32, S], F16))
    isumn = es.enter_context(sb("isumn", [86, S], F16))
    IDENT = es.enter_context(sb("IDENT", [128, 128], F16))
    GCS = es.enter_context(sb("GCS", [128, 56], F16))
    GDS = es.enter_context(sb("GDS", [128, 166], F16))
    E1S = es.enter_context(sb("E1S", [128, 256], F16))
    ONES1 = es.enter_context(sb("ONES1", [1, 128], F16))
    ZPAT8 = es.enter_context(sb("ZPAT8", [32, 8], I32))
    ZPATD = es.enter_context(sb("ZPATD", [86, 8], I32))

    sem = lambda n: es.enter_context(nc.semaphore(n))
    s_vec = sem("s_vec")
    s_pe = sem("s_pe")
    s_scl = sem("s_scl")
    s_ld = sem("s_ld")
    s_ld2 = sem("s_ld2")
    s_st = sem("s_st")
    s_pre = sem("s_pre")
    s_mm = sem("p2_mm")
    s_sig = sem("p2_sig")
    s_ist = sem("p2_ist")
    s_itc = [sem(f"p2_itc{i}") for i in range(2)]
    s_ldw = [sem("p2_ldg"), sem("p2_ldu")]
    s_ld3 = sem("p3_ld3")
    s_ld4 = sem("p3_ld4")
    s_ld5 = sem("p3_ld5")
    s_out = sem("p3_out")
    s_xt = [sem(f"p0_xt{i}") for i in range(2)]
    s_mk = [sem(f"p1_mk{i}") for i in range(6)]
    s_bex = sem("p1_bexs")
    s_wtp = [sem(f"p1_wtps{i}") for i in range(2)]
    s_xr = [sem(f"p1_xrs{i}") for i in range(4)]
    s_itb = [sem(f"p3_itbs{i}") for i in range(3)]
    s_qd = [sem(f"p3_qds{i}") for i in range(4)]
    s_zpl = sem("p3_zpl")
    s_dbg = sem("s_dbg")
    s_w16 = [sem(f"p1_w16s{i}") for i in range(2)]
    s_bias = sem("p1_bias")
    s_ast = [sem(f"p1_ast{i}") for i in range(4)]
    s_ist4 = [sem(f"p2_ist4{i}") for i in range(4)]
    s_out4 = [sem(f"p3_out4{i}") for i in range(4)]

    cnt = {'pe': 0, 'scl': 0, 'ld': 0, 'ld2': 0, 'st': 0, 'mm': 0, 'ist': 0,
           'itc': 0, 'ld3': 0, 'ld4': 0, 'ld5': 0, 'out': 0, 'ldw0': 0, 'ldw1': 0,
           'sig': 0, 'vec': 0}
    prog = {'sync': [], 'tensor': [], 'vector': [], 'scalar': [], 'gpsimd': []}
    cur_phase = [1]

    def run(name):
        ph = cur_phase[0]
        def deco(f):
            if ph <= phases:
                prog[name].append(f)
            return f
        return deco

    p0 = ExitStack()
    XT = [p0.enter_context(sb(f"p0_xt{b}", [128, D], F32)) for b in range(2)]
    XN = p0.enter_context(sb("p0_xn", [128, D], F16))
    STATS = p0.enter_context(sb("p0_stats", [128, 48], F32))
    MV = [p0.enter_context(sb(f"p0_mv{b}", [128, 2], F32)) for b in range(2)]
    VE = [p0.enter_context(sb(f"p0_ve{b}", [128, 1], F32)) for b in range(2)]
    SD = [p0.enter_context(sb(f"p0_sd{b}", [128, 1], F32)) for b in range(2)]
    RS = p0.enter_context(sb("p0_rs", [128, 1], F32))
    NMR = p0.enter_context(sb("p0_nmr", [128, 1], F32))
    MKB = [p0.enter_context(sb(f"p0_mkb{b}", [128, 1024], F16)) for b in range(6)]
    S1 = p0.enter_context(sb("p0_s1", [128, 8, 512], F16))
    S1T = XN[:].rearrange("p (a b) -> p a b", a=32)
    BM01 = p0.enter_context(sb("p0_bm01", [128, 128], F16))
    PT = [p0.enter_context(psum(f"p0_pt{b}", [128, 1024], F16)) for b in range(4)]
    S1TP = p0.enter_context(psum("p0_s1tp", [128, 1024], F16))
    S1P = [p0.enter_context(psum(f"p0_s1p{b}", [128, 512], F32)) for b in range(2)]
    XS = S1P[1][0:32, :]
    BMP = p0.enter_context(psum("p0_bmp", [128, 128], F32))
    NKS = 8

    @run('sync')
    def _(eng):
        for dst, srcap in [(IDENT[:], ident), (GCS[:], gcin), (GDS[:], gdin), (E1S[:], e1in),
                           (ONES1[:], ones1), (ZPAT8[:], zpat32), (ZPATD[:], zpatd)]:
            eng.dma_start(out=dst, in_=srcap).then_inc(s_pre, 16)
        for t in range(NT):
            if t >= 2:
                wge(eng, s_vec, M[('p0norm', t - 2)])
            eng.dma_start(out=XT[t % 2][:], in_=x[t * 128:(t + 1) * 128, :]).then_inc(s_xt[t % 2], 16)
        wge(eng, s_st, M['p1bmst'])
        for b4 in range(4):
            srcb = bmsc[b4::4, :][None, :, :].to_broadcast([32, 32, 128])
            eng.dma_start(out=BEX[32 * b4:32 * (b4 + 1), :, :], in_=srcb).then_inc(s_bex, 16)
        M['p1bex'] = 4 * 16

    @run('gpsimd')
    def _(eng):
        for k2 in range(NKS // 2 * 32):
            ks2, ncx = k2 // 32, k2 % 32
            if k2 >= 6:
                pv = k2 - 6
                wge(eng, s_pe, M[('p1s1mmB', pv // 32, pv % 32)])
            inap = mask[ncx * 128:(ncx + 1) * 128, ks2 * 1024:(ks2 + 1) * 1024]
            eng.dma_start(out=MKB[k2 % 6][:], in_=inap).then_inc(s_mk[k2 % 6], 16)
            M[('p1mk_ld', k2)] = (k2 // 6 + 1) * 16

    @run('scalar')
    def _(s):
        for t in range(NT):
            wge(s, s_vec, M[('p0ve', t)])
            s.sqrt(out=SD[t % 2][:], in_=VE[t % 2][:]).then_inc(s_scl, 1)
            cnt['scl'] += 1
            M[('p0sd', t)] = cnt['scl']
        wge(s, s_vec, M['p1bm01'])
        s.dma_start(out=bmsc, in_=BM01[:]).then_inc(s_st, 16)
        cnt['st'] += 16
        M['p1bmst'] = cnt['st']

    @run('vector')
    def _(v_raw):
        v = SerialVec(v_raw, s_vec, cnt['vec'])
        s1ks = [0]
        s1tevk = [0]

        def s1tev_upto(lim):
            while s1tevk[0] < lim:
                kc = s1tevk[0]
                v.wait_ge(s_pe, M[('p1s1t', kc)])
                v.tensor_copy(out=S1T[:, kc, :], in_=S1TP[:, (kc % 8) * 128:(kc % 8 + 1) * 128])
                M[('p1s1tev', kc)] = v.n
                s1tevk[0] += 1

        def s1ev_upto(lim):
            while s1ks[0] < lim:
                ks = s1ks[0]
                v.wait_ge(s_pe, M[('p1s1', ks)])
                v.tensor_copy(out=S1[:, ks, :], in_=S1P[ks % 2][:])
                M[('p1s1ev', ks)] = v.n
                s1ks[0] += 1

        for t in range(NT + 1):
            if t < NT:
                v.wait_ge(s_xt[t % 2], (t // 2 + 1) * 16)
                xt = XT[t % 2]
                for c8 in range(8):
                    v.bn_stats(out=STATS[:, c8 * 6:(c8 + 1) * 6], in_=xt[:, c8 * 512:(c8 + 1) * 512])
                v.bn_aggr(out=MV[t % 2][:], in_=STATS[:])
                v.tensor_scalar(out=VE[t % 2][:], in0=MV[t % 2][:, 1:2], scalar1=1e-5,
                                scalar2=None, op0=ALU.add)
                M[('p0ve', t)] = v.n
            if t >= 1:
                tp = t - 1
                xp = XT[tp % 2]
                v.wait_ge(s_scl, M[('p0sd', tp)])
                v.reciprocal(out=RS[:], in_=SD[tp % 2][:])
                v.scalar_tensor_tensor(out=NMR[:], in0=MV[tp % 2][:, 0:1], scalar=-1.0, in1=RS[:],
                                       op0=ALU.mult, op1=ALU.mult)
                if tp >= 1:
                    v.wait_ge(s_pe, M[('p0tp', tp - 1, 3)])
                v.tensor_scalar(out=XN[:], in0=xp[:], scalar1=RS[:], scalar2=NMR[:],
                                op0=ALU.mult, op1=ALU.add)
                M[('p0norm', tp)] = v.n
            if t >= 2:
                for c in range(4):
                    v.wait_ge(s_pe, M[('p0tp', t - 2, c)])
                    v.tensor_copy(out=x2T[:, 8 * c:8 * c + 8, (t - 2) * 128:(t - 1) * 128],
                                  in_=PT[c][:].rearrange("p (q s) -> p q s", q=8))
                    M[('p0ev', t - 2, c)] = v.n
                s1ev_upto(max(0, 2 * (((t - 2) * 8) // 32)))
        for c in range(4):
            v.wait_ge(s_pe, M[('p0tp', NT - 1, c)])
            v.tensor_copy(out=x2T[:, 8 * c:8 * c + 8, (NT - 1) * 128:NT * 128],
                          in_=PT[c][:].rearrange("p (q s) -> p q s", q=8))
            M[('p0ev', NT - 1, c)] = v.n
        s1ev_upto(NKS)
        for sc in range(NSC):
            v.wait_ge(s_pe, M[('p0xs', sc)])
            v.tensor_scalar(out=xsum_neg[:, sc * 512:(sc + 1) * 512], in0=XS,
                            scalar1=-1.0, scalar2=None, op0=ALU.mult)
            M[('p0xsev', sc)] = v.n
        s1tev_upto(32)
        v.wait_ge(s_pe, M['p1bmp'])
        v.tensor_scalar(out=BM01[:], in0=BMP[:], scalar1=1.0, scalar2=None, op0=ALU.min)
        M['p1bm01'] = v.n
        cnt['vec'] = v.n
        M['p0_vec_end'] = v.n

    @run('tensor')
    def _(t_):
        t_.wait_ge(s_pre, 16 * 7)
        mkit = [0]
        s1tkc = [0]

        def s1t_upto(lim):
            lim = min(lim, 8)
            while s1tkc[0] < lim:
                kc = s1tkc[0]
                wge(t_, s_vec, M[('p1s1ev', kc // 4)])
                mm = t_.transpose(out=S1TP[:, (kc % 8) * 128:(kc % 8 + 1) * 128],
                                  in_=S1[:, kc // 4, (kc % 4) * 128:(kc % 4 + 1) * 128],
                                  identity=IDENT[:])
                mm.then_inc(s_pe, 1)
                cnt['pe'] += 1
                M[('p1s1t', kc)] = cnt['pe']
                s1tkc[0] += 1

        def s1t_rest():
            while s1tkc[0] < 32:
                kc = s1tkc[0]
                wge(t_, s_vec, M[('p1s1ev', kc // 4)])
                if kc >= 8:
                    wge(t_, s_vec, M[('p1s1tev', kc - 8)])
                mm = t_.transpose(out=S1TP[:, (kc % 8) * 128:(kc % 8 + 1) * 128],
                                  in_=S1[:, kc // 4, (kc % 4) * 128:(kc % 4 + 1) * 128],
                                  identity=IDENT[:])
                mm.then_inc(s_pe, 1)
                cnt['pe'] += 1
                M[('p1s1t', kc)] = cnt['pe']
                s1tkc[0] += 1
            for kc in range(32):
                wge(t_, s_vec, M[('p1s1tev', kc)])
                mm2 = t_.matmul(out=BMP[:], lhsT=S1T[:, kc, :], rhs=E1S[:, 128 - 4 * kc:256 - 4 * kc],
                                start=(kc == 0), stop=(kc == 31))
            mm2.then_inc(s_pe, 1)
            cnt['pe'] += 1
            M['p1bmp'] = cnt['pe']

        def mask_mm_upto(limp):
            while mkit[0] < limp:
                k2 = mkit[0]
                ks2, ncx = k2 // 32, k2 % 32
                if ncx == 0 and ks2 >= 1:
                    wge(t_, s_vec, M[('p1s1ev', 2 * ks2 - 2)])
                    wge(t_, s_vec, M[('p1s1ev', 2 * ks2 - 1)])
                wge(t_, s_mk[k2 % 6], M[('p1mk_ld', k2)])
                for half in range(2):
                    mm = t_.matmul(out=S1P[half][:], lhsT=E1S[:, 128 - 4 * ncx:256 - 4 * ncx],
                                   rhs=MKB[k2 % 6][:, half * 512:(half + 1) * 512],
                                   start=(ncx == 0), stop=(ncx == 31))
                    mm.then_inc(s_pe, 1)
                    cnt['pe'] += 1
                    M[('p1s1mm' if half == 0 else 'p1s1mmB', ks2, ncx)] = cnt['pe']
                if ncx == 31:
                    M[('p1s1', 2 * ks2)] = cnt['pe'] - 1
                    M[('p1s1', 2 * ks2 + 1)] = cnt['pe']
                mkit[0] += 1

        for t in range(NT):
            mask_mm_upto(max(0, (t - 1) * 8))
            s1t_upto(max(0, min(32, 4 * ((t - 1) * 8 // 32 * 2 - 1))))
            wge(t_, s_vec, M[('p0norm', t)])
            for c in range(4):
                if t >= 1:
                    wge(t_, s_vec, M[('p0ev', t - 1, c)])
                for j in range(8):
                    mm = t_.transpose(out=PT[c][:, j * 128:(j + 1) * 128],
                                      in_=XN[:, 1024 * c + j: 1024 * c + j + 1017: 8],
                                      identity=IDENT[:])
                mm.then_inc(s_pe, 1)
                cnt['pe'] += 1
                M[('p0tp', t, c)] = cnt['pe']
        mask_mm_upto(NKS // 2 * 32)
        for sc in range(NSC):
            wge(t_, s_vec, M[('p0ev', 4 * sc + 3, 3)])
            if sc == 0:
                wge(t_, s_vec, M[('p1s1ev', NKS - 1)])
            if sc >= 1:
                wge(t_, s_vec, M[('p0xsev', sc - 1)])
            first = True
            for c in range(4):
                for j in range(8):
                    mm = t_.matmul(out=XS, lhsT=GCS[:, 24 - 8 * c:56 - 8 * c],
                                   rhs=x2T[:, 8 * c + j, sc * 512:(sc + 1) * 512],
                                   start=first, stop=(c == 3 and j == 7))
                    first = False
            mm.then_inc(s_pe, 1)
            cnt['pe'] += 1
            M[('p0xs', sc)] = cnt['pe']
        s1t_rest()
        M['p0_pe_end'] = cnt['pe']
    p0.close()

    cur_phase[0] = 2
    p1 = ExitStack()
    W16 = [p1.enter_context(sb(f"p1_w16{b}", [128, D], F16)) for b in range(2)]
    WTP = p1.enter_context(sb("p1_wtp", [128, 32, 512], F16))
    BIAS16 = p1.enter_context(sb("p1_bias16", [1, D], F16))
    XR = [p1.enter_context(sb(f"p1_xr{b}", [128, 512], F32)) for b in range(2)]
    ACCB = [p1.enter_context(sb(f"p1_accb{b}", [128, 512], F16)) for b in range(2)]
    PT1 = [p1.enter_context(psum(f"p1_pt{b}", [128, 1024], F16)) for b in range(2)]
    AT = [p1.enter_context(psum(f"p1_at{b}", [128, 512], F32)) for b in range(4)]

    @run('gpsimd')
    def _(eng):
        wge(eng, s_vec, M['p0_vec_end'])
        wge(eng, s_pe, M['p0_pe_end'])
        wge(eng, s_st, M['p1bmst'])
        eng.dma_start(out=BIAS16[:], in_=bias).then_inc(s_bias, 16)
        for nt in range(32):
            if nt >= 2:
                wge(eng, s_pe, M[('p1wt', nt - 2, 3)])
            eng.dma_start(out=W16[nt % 2][:], in_=w[nt * 128:(nt + 1) * 128, :]).then_inc(s_w16[nt % 2], 16)
            M[('p1w_ld', nt)] = (nt // 2 + 1) * 16

    @run('sync')
    def _(eng):
        wge(eng, s_vec, M['p0_vec_end'])
        wge(eng, s_pe, M['p0_pe_end'])
        wge(eng, s_st, M['p1bmst'])
        for nb in range(8):
            for st in range(NT):
                it = nb * NT + st
                if it >= 2:
                    pv = it - 2
                    wge(eng, s_vec, M[('p1acc', pv // NT, pv % NT)])
                eng.dma_start(out=XR[st % 2][:],
                              in_=x[st * 128:(st + 1) * 128, nb * 512:(nb + 1) * 512]).then_inc(s_xr[st % 2], 16)
                M[('p1xr', nb, st)] = ((nb * NT + st) // 2 + 1) * 16

    @run('vector')
    def _(v_raw):
        v = SerialVec(v_raw, s_vec, cnt['vec'])
        v.wait_ge(s_bex, M['p1bex'])

        def wm_mul(nt):
            v.wait_ge(s_w16[nt % 2], M[('p1w_ld', nt)])
            v.tensor_tensor(out=W16[nt % 2][:].rearrange("p (kb r) -> p kb r", r=32),
                            in0=W16[nt % 2][:].rearrange("p (kb r) -> p kb r", r=32),
                            in1=BEX[:, nt, :][:, :, None].to_broadcast([128, 128, 32]),
                            op=ALU.mult)
            M[('p1wm', nt)] = v.n

        def panel_evacs(nb, wm_next=False):
            for nti in range(4):
                nt = nb * 4 + nti
                for c in range(4):
                    v.wait_ge(s_pe, M[('p1wt', nt, c)])
                    v.tensor_copy(out=WTP[:, 8 * c:8 * c + 8, nti * 128:(nti + 1) * 128],
                                  in_=PT1[c % 2][:].rearrange("p (q n) -> p q n", q=8))
                    M[('p1wtev', nt, c)] = v.n
                if wm_next and nti < 2:
                    wm_mul(nb * 4 + nti + 2)

        wm_mul(0)
        wm_mul(1)
        panel_evacs(0, wm_next=True)
        for nb in range(8):
            for st in range(NT):
                if nb + 1 < 8 and st in (3, 7):
                    wm_mul((nb + 1) * 4 + (st - 3) // 4)
                v.wait_ge(s_pe, M[('p1at', nb, st)])
                v.wait_ge(s_xr[st % 2], M[('p1xr', nb, st)])
                if nb * NT + st >= 2:
                    pv = nb * NT + st - 2
                    v.wait_ge(s_ast[pv % 4], (pv // 4 + 1) * 16)
                v.tensor_tensor(out=ACCB[st % 2][:], in0=AT[st % 4][:], in1=XR[st % 2][:], op=ALU.add)
                M[('p1acc', nb, st)] = v.n
            if nb + 1 < 8:
                panel_evacs(nb + 1, wm_next=True)
        cnt['vec'] = v.n
        M['p1_vec_end'] = v.n

    @run('tensor')
    def _(t_):
        t_.wait_ge(s_bias, 16)
        for nb in range(8):
            for nti in range(4):
                nt = nb * 4 + nti
                wge(t_, s_vec, M[('p1wm', nt)])
                for c in range(4):
                    if nt * 4 + c >= 2:
                        pv = nt * 4 + c - 2
                        wge(t_, s_vec, M[('p1wtev', pv // 4, pv % 4)])
                    for j in range(8):
                        mm = t_.transpose(out=PT1[c % 2][:, j * 128:(j + 1) * 128],
                                          in_=W16[nt % 2][:, 1024 * c + j: 1024 * c + j + 1017: 8],
                                          identity=IDENT[:])
                    mm.then_inc(s_pe, 1)
                    cnt['pe'] += 1
                    M[('p1wt', nt, c)] = cnt['pe']
            wge(t_, s_vec, M[('p1wtev', nb * 4 + 3, 3)])
            for st in range(NT):
                if nb * NT + st >= 4:
                    pv = nb * NT + st - 4
                    wge(t_, s_vec, M[('p1acc', pv // NT, pv % NT)])
                t_.matmul(out=AT[st % 4][:], lhsT=ONES1[:], rhs=BIAS16[:, nb * 512:(nb + 1) * 512],
                          start=True, stop=False)
                for c in range(4):
                    for j in range(8):
                        q = 8 * c + j
                        mm = t_.matmul(out=AT[st % 4][:], lhsT=x2T[:, q, st * 128:(st + 1) * 128],
                                       rhs=WTP[:, q, :], start=False, stop=(c == 3 and j == 7))
                mm.then_inc(s_pe, 1)
                cnt['pe'] += 1
                M[('p1at', nb, st)] = cnt['pe']
        M['p1_pe_end'] = cnt['pe']

    @run('scalar')
    def _(s):
        for nb in range(8):
            for st in range(NT):
                wge(s, s_vec, M[('p1acc', nb, st)])
                s.dma_start(out=acc[st * 128:(st + 1) * 128, nb * 512:(nb + 1) * 512],
                            in_=ACCB[st % 2][:]).then_inc(s_ast[st % 4], 16)
        M['p1_st_end'] = cnt['st']
    p1.close()

    cur_phase[0] = 3
    p2 = ExitStack()
    Q = [p2.enter_context(sb(f"p2_q{ww}", [128, 512], I32)) for ww in range(2)]
    SX = [p2.enter_context(sb(f"p2_sx{ww}", [128, 512], F16)) for ww in range(2)]
    SSB = [p2.enter_context(sb(f"p2_ss{ww}", [32, 128], F16)) for ww in range(2)]
    ZSB = [p2.enter_context(sb(f"p2_zs{ww}", [32, 16], I32)) for ww in range(2)]
    ZPB = [p2.enter_context(sb(f"p2_zp{ww}", [32, 128], F16)) for ww in range(2)]
    ZPI = p2.enter_context(sb("p2_zpi", [32, 128], I32))
    NIB = p2.enter_context(sb("p2_nib", [128, 512], I32))
    PL = [p2.enter_context(sb(f"p2_pl{ww}", [128, 8, 512], F16)) for ww in range(2)]
    GATE = p2.enter_context(sb("p2_gate", [128, NSC, 512], F16))
    UPB = p2.enter_context(sb("p2_upb", [128, NSC, 512], F16))
    SIGB = p2.enter_context(sb("p2_sig", [128, NSC, 512], F16))
    T1 = p2.enter_context(sb("p2_t1", [128, 512], F32))
    INT = [p2.enter_context(sb(f"p2_int{b}", [128, 512], F16)) for b in range(4)]
    ITC = [p2.enter_context(sb(f"p2_itc{b}", [128, 8, 512], F16)) for b in range(2)]
    ACC2 = [p2.enter_context(psum(f"p2_acc{i}", [128, 512], F32)) for i in range(4)]
    ISUMP = [p2.enter_context(psum(f"p2_isum{i}", [86, 512], F32)) for i in range(4)]

    qw_ = [gq, uq]
    sxw_ = [gsx, usx]
    sw_ = [gs, us]
    zw_ = [gz, uz]
    MM2 = lambda ii, ww, sc: (ii * 2 + ww) * NSC + sc + 1
    PSC2 = lambda ii, sc: ii * NSC + sc + 1

    @run('sync')
    def _(eng):
        wge(eng, s_vec, M['p1_vec_end'])
        wge(eng, s_pe, M['p1_pe_end'])
        for sl in range(4):
            eng.wait_ge(s_ast[sl], 32 * 16)
        for ii in range(NI):
            for ww in range(2):
                if ii >= 1:
                    wge(eng, s_vec, M[('p2deq', ii - 1, ww)])
                eng.dma_start(out=Q[ww][:].rearrange("p (c i) -> p c i", c=4),
                              in_=qw_[ww].rearrange("(c p) i -> p c i", p=128)[:, :, ii * 128:(ii + 1) * 128]).then_inc(s_ldw[ww], 16)
                eng.dma_start(out=SX[ww][:].rearrange("p (c i) -> p c i", c=4),
                              in_=sxw_[ww].rearrange("(c p) i -> p c i", p=128)[:, :, ii * 128:(ii + 1) * 128]).then_inc(s_ldw[ww], 16)
                eng.dma_start(out=SSB[ww][:], in_=sw_[ww][:, ii * 128:(ii + 1) * 128]).then_inc(s_ldw[ww], 16)
                eng.dma_start(out=ZSB[ww][:], in_=zw_[ww][:, ii * 16:(ii + 1) * 16]).then_inc(s_ldw[ww], 16)


    @run('gpsimd')
    def _(eng):
        for cp in range(NCD):
            rows = 128 if cp < 10 else 96
            hi = min((8 * cp + 8), NI) * 16
            for sc in range(NSC):
                k = cp * NSC + sc
                for sl in range(4):
                    wge(eng, s_ist4[sl], hi)
                if k >= 2:
                    pv = k - 2
                    wge(eng, s_pe, M[('p2is', pv // NSC, pv % NSC)])
                inap = interT[cp * 1024: cp * 1024 + rows * 8, sc * 512:(sc + 1) * 512]
                inap = inap.rearrange("(p j) s -> p j s", j=8)
                eng.dma_start(out=ITC[k % 2][0:rows, :, :], in_=inap).then_inc(s_itc[k % 2], 16)
                M[('p2itc', cp, sc)] = (k // 2 + 1) * 16

    @run('vector')
    def _(v_raw):
        v = SerialVec(v_raw, s_vec, cnt['vec'])

        def deq2(ii, ww):
            v.wait_ge(s_ldw[ww], (ii + 1) * 64)
            zin = ZSB[ww][:, :, None].to_broadcast([32, 16, 8])
            v.tensor_tensor(out=ZPI[:].rearrange("g (a c) -> g a c", c=8), in0=zin,
                            in1=ZPAT8[:, None, :].to_broadcast([32, 16, 8]), op=ALU.logical_shift_right)
            v.tensor_scalar(out=ZPI[:], in0=ZPI[:], scalar1=15, scalar2=None, op0=ALU.bitwise_and)
            v.scalar_tensor_tensor(out=ZPB[ww][:], in0=ZPI[:], scalar=1.0,
                                   in1=SSB[ww][:], op0=ALU.add, op1=ALU.mult)
            for j in range(8):
                v.tensor_scalar(out=NIB[:], in0=Q[ww][:], scalar1=4 * j, scalar2=15,
                                op0=ALU.logical_shift_right, op1=ALU.bitwise_and)
                v.scalar_tensor_tensor(out=PL[ww][:, j, :], in0=NIB[:], scalar=1.0,
                                       in1=SX[ww][:], op0=ALU.mult, op1=ALU.mult)
            M[('p2deq', ii, ww)] = v.n

        deq2(0, 0)
        for ii in range(NI):
            deq2(ii, 1)
            for sc in range(NSC):
                v.wait_ge(s_mm, MM2(ii, 0, sc))
                v.tensor_copy(out=GATE[:, sc, :], in_=ACC2[sc][:])
                M[('p2gev', ii, sc)] = v.n
            if ii + 1 < NI:
                deq2(ii + 1, 0)
            for sc in range(NSC):
                v.wait_ge(s_mm, MM2(ii, 1, sc))
                v.tensor_copy(out=UPB[:, sc, :], in_=ACC2[sc][:])
                M[('p2uev', ii, sc)] = v.n
            for sc in range(NSC):
                slot = (ii * NSC + sc) % 4
                v.wait_ge(s_sig, PSC2(ii, sc))
                v.tensor_tensor(out=T1[:], in0=GATE[:, sc, :], in1=SIGB[:, sc, :], op=ALU.mult)
                if ii * NSC + sc >= 4:
                    v.wait_ge(s_ist4[slot], ((ii * NSC + sc) // 4) * 16)
                v.tensor_tensor(out=INT[slot][:], in0=T1[:], in1=UPB[:, sc, :], op=ALU.mult)
                M[('p2int', ii, sc)] = v.n
        for sc in range(NSC):
            v.wait_ge(s_pe, M[('p2is', NCD - 1, sc)])
            v.tensor_scalar(out=isumn[:, sc * 512:(sc + 1) * 512], in0=ISUMP[sc][:],
                            scalar1=-1.0, scalar2=None, op0=ALU.mult)
            M[('p2isev', sc)] = v.n
        cnt['vec'] = v.n
        M['p2_vec_end'] = v.n

    def _p2_isum(t_, cp):
        rows = 128 if cp < 10 else 96
        for sc in range(NSC):
            wge(t_, s_itc[(cp * NSC + sc) % 2], M[('p2itc', cp, sc)])
            for j in range(8):
                mm = t_.matmul(out=ISUMP[sc][:], lhsT=GDS[0:rows, 80 - 8 * cp:166 - 8 * cp],
                               rhs=ITC[(cp * NSC + sc) % 2][0:rows, j, :],
                               start=(cp == 0 and j == 0), stop=(cp == NCD - 1 and j == 7))
            mm.then_inc(s_pe, 1)
            cnt['pe'] += 1
            M[('p2is', cp, sc)] = cnt['pe']

    @run('tensor')
    def _(t_):
        for ii in range(NI):
            for ww in range(2):
                wge(t_, s_vec, M[('p2deq', ii, ww)])
                for sc in range(NSC):
                    if ww == 1:
                        wge(t_, s_vec, M[('p2gev', ii, sc)])
                    elif ii >= 1:
                        wge(t_, s_vec, M[('p2uev', ii - 1, sc)])
                    first = True
                    for c in range(4):
                        for j in range(8):
                            t_.matmul(out=ACC2[sc][:], lhsT=PL[ww][:, j, c * 128:(c + 1) * 128],
                                      rhs=x2T[:, 8 * c + j, sc * 512:(sc + 1) * 512],
                                      start=first, stop=False)
                            first = False
                    t_.matmul(out=ACC2[sc][:], lhsT=ZPB[ww][:],
                              rhs=xsum_neg[:, sc * 512:(sc + 1) * 512],
                              start=False, stop=True).then_inc(s_mm, 1)
                    cnt['mm'] += 1
            if ii % 8 == 7 and ii // 8 >= 1:
                _p2_isum(t_, ii // 8 - 1)
        for cp in [9, 10]:
            _p2_isum(t_, cp)
        M['p2_pe_end'] = cnt['pe']

    @run('scalar')
    def _(s):
        for ii in range(NI):
            for sc in range(NSC):
                wge(s, s_vec, M[('p2gev', ii, sc)])
                s.activation(out=SIGB[:, sc, :], in_=GATE[:, sc, :], func=AF.Sigmoid).then_inc(s_sig, 1)
                cnt['sig'] += 1
            for sc in range(NSC):
                slot = (ii * NSC + sc) % 4
                wge(s, s_vec, M[('p2int', ii, sc)])
                s.dma_start(out=interT[ii * 128:(ii + 1) * 128, sc * 512:(sc + 1) * 512],
                            in_=INT[slot][:]).then_inc(s_ist4[slot], 16)
    p2.close()

    cur_phase[0] = 4
    x2Tes.close()
    NQD, NIT, NPLD = 4, 3, 8
    p3 = ExitStack()
    ITB = [p3.enter_context(sb(f"p3_itb{b}", [128, 8, 1024], F16)) for b in range(NIT)]
    QD = [p3.enter_context(sb(f"p3_qd{b}", [128, 512], I32)) for b in range(NQD)]
    DSE = [p3.enter_context(sb(f"p3_dse{b}", [128, 512], F16)) for b in range(NQD)]
    PLD = [p3.enter_context(sb(f"p3_pld{b}", [128, 512], F16)) for b in range(NPLD)]
    NIBD = p3.enter_context(sb("p3_nibd", [128, 512], I32))
    SDS = p3.enter_context(sb("p3_sds", [86, 512], F16))
    ZDS = p3.enter_context(sb("p3_zds", [86, 64], I32))
    ZDI = p3.enter_context(sb("p3_zdi", [86, 512], I32))
    ZPD = p3.enter_context(sb("p3_zpd", [86, 512], F16))
    ACC3 = p3.enter_context(sb("p3_acc", [128, 8, 512], F16))
    OUTB = [p3.enter_context(sb(f"p3_outb{b}", [128, 512], F32)) for b in range(4)]
    OUT = [p3.enter_context(psum(f"p3_out{b}", [128, 512], F32)) for b in range(8)]

    @run('sync')
    def _(eng):
        wge(eng, s_vec, M['p2_vec_end'])
        wge(eng, s_pe, M['p2_pe_end'])
        for sl in range(4):
            eng.wait_ge(s_ist4[sl], NI * 16)
        qd_prev = {i: None for i in range(NQD)}
        itb_prev = {i: None for i in range(NIT)}
        qd_cnt = {i: 0 for i in range(NQD)}
        itb_cnt = {i: 0 for i in range(NIT)}
        for hb in range(8):
            if hb >= 1:
                wge(eng, s_vec, M[('p3zpd', hb - 1)])
            eng.dma_start(out=SDS[:], in_=ds[:, hb * 512:(hb + 1) * 512]).then_inc(s_zpl, 16)
            eng.dma_start(out=ZDS[:], in_=dz[:, hb * 64:(hb + 1) * 64]).then_inc(s_zpl, 16)
            M[('p3zp_ld', hb)] = (hb + 1) * 32
            for sh in range(2):
                for cp in range(NCD):
                    rows = 128 if cp < 10 else 96
                    it = (hb * 2 + sh) * NCD + cp
                    sl = it % NQD
                    prevq = qd_prev[sl]
                    if prevq is not None:
                        wge(eng, s_vec, M[('p3deq', *prevq)])
                        wge(eng, s_pe, M[('p3mm', *prevq)])
                    qd_prev[sl] = (hb, sh, cp)
                    eng.dma_start(out=QD[sl][0:rows, :],
                                  in_=dq[cp * 128:cp * 128 + rows, hb * 512:(hb + 1) * 512]).then_inc(s_qd[sl], 16)
                    eng.dma_start(out=DSE[sl][0:rows, :],
                                  in_=dsx[cp * 128:cp * 128 + rows, hb * 512:(hb + 1) * 512]).then_inc(s_qd[sl], 16)
                    qd_cnt[sl] += 32
                    M[('p3qd_ld', hb, sh, cp)] = qd_cnt[sl]
                    sli = it % NIT
                    previ = itb_prev[sli]
                    if previ is not None:
                        wge(eng, s_pe, M[('p3mm', *previ)])
                    itb_prev[sli] = (hb, sh, cp)
                    inap = interT[cp * 1024:cp * 1024 + rows * 8, sh * 1024:(sh + 1) * 1024]
                    inap = inap.rearrange("(p j) s -> p j s", j=8)
                    eng.dma_start(out=ITB[sli][0:rows, :, :], in_=inap).then_inc(s_itb[sli], 16)
                    itb_cnt[sli] += 16
                    M[('p3itb', hb, sh, cp)] = itb_cnt[sli]
        for sl in range(4):
            eng.wait_ge(s_out4[sl], 32 * 16)

    @run('gpsimd')
    def _(eng):
        wge(eng, s_vec, M['p2_vec_end'])
        for sl in range(4):
            eng.wait_ge(s_ast[sl], 32 * 16)
            eng.wait_ge(s_ist4[sl], NI * 16)
        for hb in range(8):
            for sh in range(2):
                it = hb * 2 + sh
                if it >= 1:
                    pv = it - 1
                    wge(eng, s_vec, M[('p3ev', pv // 2, pv % 2, 7)])
                inap = acc[sh * 1024:(sh + 1) * 1024, hb * 512:(hb + 1) * 512].rearrange("(st p) h -> p st h", st=8)
                eng.dma_start(out=ACC3[:], in_=inap).then_inc(s_ld5, 16)
                cnt['ld5'] += 16
                M[('p3acc_ld', hb, sh)] = cnt['ld5']

    @run('vector')
    def _(v_raw):
        v = SerialVec(v_raw, s_vec, cnt['vec'])

        def zpd3(hb):
            v.wait_ge(s_zpl, M[('p3zp_ld', hb)])
            zin = ZDS[:, :, None].to_broadcast([86, 64, 8])
            pin = ZPATD[:, None, :].to_broadcast([86, 64, 8])
            v.tensor_tensor(out=ZDI[:].rearrange("g (a c) -> g a c", c=8), in0=zin, in1=pin,
                            op=ALU.logical_shift_right)
            v.tensor_scalar(out=ZDI[:], in0=ZDI[:], scalar1=15, scalar2=None, op0=ALU.bitwise_and)
            v.scalar_tensor_tensor(out=ZPD[:], in0=ZDI[:], scalar=1.0, in1=SDS[:],
                                   op0=ALU.add, op1=ALU.mult)
            M[('p3zpd', hb)] = v.n

        def deq3(H, cp):
            hb, sh = H // 2, H % 2
            rows = 128 if cp < 10 else 96
            it = H * NCD + cp
            v.wait_ge(s_qd[it % NQD], M[('p3qd_ld', hb, sh, cp)])
            for j in range(8):
                slot = (it * 8 + j) % NPLD
                pj = it * 8 + j - NPLD
                if pj >= 0:
                    key = ('p3mmj', pj // (NCD * 8 * 2), (pj // (NCD * 8)) % 2,
                           (pj % (NCD * 8)) // 8, pj % 8)
                    v.wait_ge(s_pe, M[key])
                v.tensor_scalar(out=NIBD[0:rows, :], in0=QD[it % NQD][0:rows, :],
                                scalar1=4 * j, scalar2=15,
                                op0=ALU.logical_shift_right, op1=ALU.bitwise_and)
                v.scalar_tensor_tensor(out=PLD[slot][0:rows, :], in0=NIBD[0:rows, :],
                                       scalar=1.0, in1=DSE[it % NQD][0:rows, :],
                                       op0=ALU.mult, op1=ALU.mult)
                M[('p3deqj', hb, sh, cp, j)] = v.n
            M[('p3deq', hb, sh, cp)] = v.n

        zpd3(0)
        deq3(0, 0)
        for H in range(16):
            hb, sh = H // 2, H % 2
            for cp in range(1, NCD):
                deq3(H, cp)
            if H + 1 < 16:
                deq3(H + 1, 0)
            for st in range(8):
                v.wait_ge(s_pe, M[('p3corr', hb, sh, st)])
                v.wait_ge(s_ld5, M[('p3acc_ld', hb, sh)])
                slot = st % 4
                gidx = H * 8 + st
                if gidx >= 4:
                    v.wait_ge(s_out4[gidx % 4], (gidx // 4) * 16)
                v.tensor_tensor(out=OUTB[slot][:], in0=OUT[st][:], in1=ACC3[:, st, :],
                                op=ALU.add)
                M[('p3ev', hb, sh, st)] = v.n
            if H + 1 < 16 and (H + 1) % 2 == 0:
                zpd3((H + 1) // 2)
        cnt['vec'] = v.n

    @run('tensor')
    def _(t_):
        for hb in range(8):
            for sh in range(2):
                for cp in range(NCD):
                    rows = 128 if cp < 10 else 96
                    it = (hb * 2 + sh) * NCD + cp
                    wge(t_, s_itb[it % NIT], M[('p3itb', hb, sh, cp)])
                    for j in range(8):
                        slot = (it * 8 + j) % NPLD
                        wge(t_, s_vec, M[('p3deqj', hb, sh, cp, j)])
                        for st in range(8):
                            if cp == 0 and j == 0 and hb * 2 + sh >= 1:
                                pv = hb * 2 + sh - 1
                                wge(t_, s_vec, M[('p3ev', pv // 2, pv % 2, st)])
                            mm = t_.matmul(out=OUT[st][:],
                                           lhsT=ITB[it % NIT][0:rows, j, st * 128:(st + 1) * 128],
                                           rhs=PLD[slot][0:rows, :],
                                           start=(cp == 0 and j == 0), stop=False)
                        mm.then_inc(s_pe, 1)
                        cnt['pe'] += 1
                        M[('p3mmj', hb, sh, cp, j)] = cnt['pe']
                    M[('p3mm', hb, sh, cp)] = cnt['pe']
                wge(t_, s_vec, M[('p3zpd', hb)])
                if hb == 0 and sh == 0:
                    for sc in range(NSC):
                        wge(t_, s_vec, M[('p2isev', sc)])
                for st in range(8):
                    gcol = sh * 1024 + st * 128
                    mm = t_.matmul(out=OUT[st][:], lhsT=isumn[:, gcol:gcol + 128], rhs=ZPD[:],
                                   start=False, stop=True)
                    mm.then_inc(s_pe, 1)
                    cnt['pe'] += 1
                    M[('p3corr', hb, sh, st)] = cnt['pe']

    @run('scalar')
    def _(s):
        for hb in range(8):
            for sh in range(2):
                for st in range(8):
                    wge(s, s_vec, M[('p3ev', hb, sh, st)])
                    slot = st % 4
                    s.dma_start(out=out[sh * 1024 + st * 128: sh * 1024 + (st + 1) * 128,
                                        hb * 512:(hb + 1) * 512],
                                in_=OUTB[slot][:]).then_inc(s_out4[(hb * 16 + sh * 8 + st) % 4], 16)
    p3.close()

    cur_phase[0] = 1
    if debug:
        x2Td = dt("x2Td", [128, 32, S], F16, kind="ExternalOutput")
        xsnd = dt("xsnd", [32, S], F16, kind="ExternalOutput")
        isnd = dt("isnd", [86, S], F16, kind="ExternalOutput")

        @run('gpsimd')
        def _(eng):
            wge(eng, s_vec, M['p0_vec_end'])
            if phases >= 2:
                for sl in range(4):
                    eng.wait_ge(s_ast[sl], 32 * 16)
            if phases >= 3:
                for sl in range(4):
                    eng.wait_ge(s_ist4[sl], NI * 16)
                wge(eng, s_vec, M['p2_vec_end'])
            if phases >= 4:
                for sl in range(4):
                    eng.wait_ge(s_out4[sl], 32 * 16)
            ndma = 0
            if phases < 4:
                eng.dma_start(out=x2Td, in_=x2T[:]).then_inc(s_dbg, 16)
                ndma += 1
            eng.dma_start(out=xsnd, in_=xsum_neg[:]).then_inc(s_dbg, 16)
            ndma += 1
            if phases >= 3:
                eng.dma_start(out=isnd, in_=isumn[:]).then_inc(s_dbg, 16)
                ndma += 1
            eng.wait_ge(s_dbg, 16 * ndma)

    blk = es.enter_context(nc.Block())

    @blk.vector
    def _(v):
        for f in prog['vector']:
            f(v)

    @blk.tensor
    def _(t):
        for f in prog['tensor']:
            f(t)

    @blk.scalar
    def _(s):
        for f in prog['scalar']:
            f(s)

    @blk.sync
    def _(sy):
        for f in prog['sync']:
            f(sy)

    @blk.gpsimd
    def _(g):
        for f in prog['gpsimd']:
            f(g)

    es.close()
    return nc


def make_consts():
    f16 = np.float16
    ident = np.eye(128, dtype=np.float32).astype(f16)
    gc = np.zeros((128, 56), dtype=np.float32)
    e1 = np.zeros((128, 256), dtype=np.float32)
    gd = np.zeros((128, 166), dtype=np.float32)
    for p in range(128):
        gc[p, 24 + p // 16] = 1.0
        e1[p, 128 + p // 32] = 1.0
        gd[p, 80 + p // 16] = 1.0
    ones1 = np.ones((1, 128), dtype=np.float32)
    zpat32 = np.tile((np.arange(8) * 4).astype(np.int32), (32, 1))
    zpatd = np.tile((np.arange(8) * 4).astype(np.int32), (86, 1))
    return {"ident": ident, "gc": gc.astype(f16), "e1": e1.astype(f16),
            "gd": gd.astype(f16), "ones1": ones1.astype(f16),
            "zpat32": zpat32, "zpatd": zpatd}


def kernel(L0_attn_data, L0_attn_mask, L0_attn_weight, L0_attn_bias,
           L0_ffn_gate_qweight, L0_ffn_gate_scales, L0_ffn_gate_zeros,
           L0_ffn_up_qweight, L0_ffn_up_scales, L0_ffn_up_zeros,
           L0_ffn_down_qweight, L0_ffn_down_scales, L0_ffn_down_zeros,
           _debug=False, _phases=4):
    xfull = np.ascontiguousarray(np.asarray(L0_attn_data, dtype=np.float32).reshape(16384, D))
    shared = {
        "mask": np.ascontiguousarray(np.asarray(L0_attn_mask, dtype=np.int32)),
        "w": np.ascontiguousarray(np.asarray(L0_attn_weight, dtype=np.float32)),
        "bias": np.ascontiguousarray(np.asarray(L0_attn_bias, dtype=np.float32).reshape(1, D)),
        "gate_q": np.ascontiguousarray(np.asarray(L0_ffn_gate_qweight)),
        "gate_sx": np.ascontiguousarray(np.repeat(np.asarray(L0_ffn_gate_scales), 16, axis=0)),
        "gate_s": np.ascontiguousarray(np.asarray(L0_ffn_gate_scales)),
        "gate_z": np.ascontiguousarray(np.asarray(L0_ffn_gate_zeros)),
        "up_q": np.ascontiguousarray(np.asarray(L0_ffn_up_qweight)),
        "up_sx": np.ascontiguousarray(np.repeat(np.asarray(L0_ffn_up_scales), 16, axis=0)),
        "up_s": np.ascontiguousarray(np.asarray(L0_ffn_up_scales)),
        "up_z": np.ascontiguousarray(np.asarray(L0_ffn_up_zeros)),
        "down_q": np.ascontiguousarray(np.asarray(L0_ffn_down_qweight)),
        "down_sx": np.ascontiguousarray(np.repeat(np.asarray(L0_ffn_down_scales), 16, axis=0)),
        "down_s": np.ascontiguousarray(np.asarray(L0_ffn_down_scales)),
        "down_z": np.ascontiguousarray(np.asarray(L0_ffn_down_zeros)),
    }
    shared.update(make_consts())
    M = MDict()
    build_nc(M, debug=_debug, phases=_phases)
    nc = build_nc(dict(M), debug=_debug, phases=_phases)
    in_maps = []
    for c in range(NCORES):
        m = dict(shared)
        m["x"] = np.ascontiguousarray(xfull[c * S:(c + 1) * S, :])
        in_maps.append(m)
    res = run_bass_kernel_spmd(nc, in_maps, core_ids=list(range(NCORES)))
    outs = [res.results[c]["out"] for c in range(NCORES)]
    full = np.concatenate(outs, axis=0).reshape(1, 16384, D).astype(np.float32)
    if _debug:
        return full, res
    return full
